# version 10
# baseline (speedup 1.0000x reference)
import contextlib
import numpy as np
import concourse.bass as bass
import concourse.mybir as mybir
from concourse.bass_utils import run_bass_kernel_spmd

F32 = mybir.dt.float32
BF16 = mybir.dt.bfloat16
ALU = mybir.AluOpType
AF = mybir.ActivationFunctionType
AXX = mybir.AxisListType.X

DEBUG = False

NCORES = 8
DM = 1024
SEQ = 16384
TOK = SEQ // NCORES
T = 128
NCH = TOK // T
SC = 512
NSC = TOK // SC
DIN = 2048
NPROJ = 4385
NPC = 35
EPS = 1e-5
DIRS = ("f", "r")


class Ctr:
    def __init__(self, es, nc, name):
        self.h = es.enter_context(nc.semaphore(name))
        self.n = 0

    def inc(self, instr, k=1):
        instr.then_inc(self.h, k)
        self.n += k
        return instr

    def dma(self, instr):
        return self.inc(instr, 16)


def dmaq(ctr, eng, **kw):
    eng.wait_ge(ctr.h, ctr.n)
    return ctr.dma(eng.dma_start(**kw))


def build_program():
    nc = bass.Bass("TRN2", target_bir_lowering=False, debug=False)

    def din(name, shape, dt=F32):
        return nc.dram_tensor(name, shape, dt, kind="ExternalInput").ap()

    def dsc(name, shape, dt=F32):
        return nc.dram_tensor(name, shape, dt).ap()

    xin = din("xin", [TOK + 6, DM])
    wiT = din("wiT", [DM, NPROJ])
    w1 = din("w1", [4, DM, 4096])
    w2 = din("w2", [4, 4096, DM])
    b1c = din("b1c", [128, 128])
    b2r = din("b2r", [1, 4096])
    woT = din("woT", [DIN, DM])
    cw = {"f": din("cwf", [128, 72]), "r": din("cwr", [128, 72])}
    cb = {"f": din("cbf", [128, 18]), "r": din("cbr", [128, 18])}
    dbias = {"f": din("dbf", [32, 1]), "r": din("dbr", [32, 1])}
    alog = {"f": din("alf", [32, 1]), "r": din("alr", [32, 1])}
    dcol = {"f": din("dcolf", [128, 16]), "r": din("dcolr", [128, 16])}
    nwin = {"f": din("nwf", [128, 16]), "r": din("nwr", [128, 16])}
    bu1 = din("bu1", [1, 1])
    wu_row = din("wu_row", [1, DM])
    thr4 = din("thr4", [4, TOK])
    rec4 = din("rec4", [4, TOK])
    maskcs = din("maskcs", [32, TOK + 1])
    ident = din("ident", [128, 128])
    identbf = din("identbf", [128, 128], BF16)
    itile = din("itile", [128, DIN])
    m8 = {"f": din("m8f", [8, 1]), "r": din("m8r", [8, 1])}
    mk2 = {"f": din("mk2f", [8, 8]), "r": din("mk2r", [8, 8])}

    out = nc.dram_tensor("out", [TOK, DM], F32, kind="ExternalOutput").ap()

    d_z = dsc("d_z", [DIN, TOK])
    d_xscm = {d: dsc(f"d_xscm_{d}", [DIN, TOK]) for d in DIRS}
    d_xd = {d: dsc(f"d_xd_{d}", [NCH, 128, DIN], BF16) for d in DIRS}
    d_cs = {d: dsc(f"d_cs_{d}", [NCH, 128, DIN], BF16) for d in DIRS}
    d_B = {d: dsc(f"d_B_{d}", [128, TOK], BF16) for d in DIRS}
    d_C = {d: dsc(f"d_C_{d}", [128, TOK], BF16) for d in DIRS}
    d_rrow = {d: dsc(f"d_rrow_{d}", [32, TOK]) for d in DIRS}
    d_erow = {d: dsc(f"d_erow_{d}", [32, TOK], BF16) for d in DIRS}
    d_dch = {d: dsc(f"d_dch_{d}", [NCH, 32]) for d in DIRS}
    d_uraw = dsc("d_uraw", [1, TOK])
    d_u = dsc("d_u", [1, TOK])
    d_c4 = dsc("d_c4", [4, TOK], BF16)
    d_rstd = {d: dsc(f"d_rstd_{d}", [1, TOK]) for d in DIRS}
    d_g = dsc("d_g", [DM, TOK])
    d_w1bf = dsc("d_w1bf", [4, 32, 8, 128, 128], BF16)
    d_w2bf = dsc("d_w2bf", [4, 8, 32, 128, 128], BF16)
    d_wobf = {d: dsc(f"d_wobf_{d}", [8, 16, 128, 128], BF16) for d in DIRS}
    d_winbf = dsc("d_winbf", [NPC, 8, 128, 128], BF16)
    d_Sr = dsc("d_Sr", [NCH, 128, DIN], BF16)
    d_w8 = dsc("d_w8", [8, 64])
    cc_in = dsc("cc_in", [257, DIN])
    cc_out = dsc("cc_out", [257 * 8, DIN])

    dbg = {}
    if DEBUG:
        def dout(name, shape, dt=F32):
            dbg[name] = nc.dram_tensor(name, shape, dt, kind="ExternalOutput").ap()
        dout("o_xn0", [128, TOK + 6], BF16)
        dout("o_redc", [128, 4]); dout("o_xsq", [128, DM])
        dout("o_xnb", [128, DM], BF16); dout("o_udot", [128, DM])
        dout("o_z", [256, TOK])
        dout("o_rrow_f", [32, TOK]); dout("o_rrow_r", [32, TOK])
        dout("o_erow_f", [32, TOK], BF16); dout("o_erow_r", [32, TOK], BF16)
        dout("o_dch_f", [NCH, 32]); dout("o_dch_r", [NCH, 32])
        dout("o_xd_f", [128, DIN], BF16); dout("o_xd_r", [128, DIN], BF16)
        dout("o_cs_f", [128, DIN], BF16); dout("o_cs_r", [128, DIN], BF16)
        dout("o_xscm_f", [256, TOK])
        dout("o_B_f", [128, TOK], BF16); dout("o_C_f", [128, TOK], BF16)
        dout("o_xbc16", [128, 518]); dout("o_xbc17", [128, 518])
        dout("o_ccin", [257, DIN])
        dout("o_u", [1, TOK]); dout("o_c4", [4, TOK], BF16)
        dout("o_g", [256, TOK])
        dout("o_ych_f", [128, DIN]); dout("o_ych_r", [128, DIN])
        dout("o_rstd_f", [1, TOK]); dout("o_rstd_r", [1, TOK])
        dout("o_sr15", [128, DIN], BF16)
        dout("o_inc_f", [128, DIN]); dout("o_inc_r", [128, DIN])

    es = contextlib.ExitStack()
    with es:
        def sb(name, shape, dt=F32):
            return es.enter_context(nc.sbuf_tensor(name, shape, dt))

        s_id = sb("c_id", [128, 128])
        s_idbf = sb("c_idbf", [128, 128], BF16)
        s_cw = {d: sb(f"c_cw{d}", [128, 72]) for d in DIRS}
        s_cb = {d: sb(f"c_cb{d}", [128, 18]) for d in DIRS}
        s_db = {d: sb(f"c_db{d}", [32, 1]) for d in DIRS}
        s_eA = {d: sb(f"c_eA{d}", [32, 1]) for d in DIRS}
        s_dcol = {d: sb(f"c_dcol{d}", [128, 16]) for d in DIRS}
        s_nw = {d: sb(f"c_nw{d}", [128, 16]) for d in DIRS}
        s_b1c = sb("c_b1c", [128, 128])
        s_b2bf = sb("c_b2bf", [1, 4096], BF16)
        s_ones = sb("c_onesbf", [128, 1], BF16)
        s_bu = sb("c_bu", [1, 1])
        s_m8 = {d: sb(f"c_m8{d}", [8, 1]) for d in DIRS}
        s_mk2 = {d: sb(f"c_mk2{d}", [8, 8]) for d in DIRS}
        s_S = {"f": sb("s_Sf", [128, DIN]), "r": sb("s_Sr", [128, DIN])}
        s_cvals = sb("c_cvals", [128, 2])
        s_absum = {d: sb(f"s_absum{d}", [32, 1]) for d in DIRS}

        sQ = Ctr(es, nc, "sQ")
        sG = Ctr(es, nc, "sG")
        sSc = Ctr(es, nc, "sSc")
        sPE = Ctr(es, nc, "sPE")
        sV = Ctr(es, nc, "sV")
        sA = Ctr(es, nc, "sA")
        sCC = Ctr(es, nc, "sCC")
        stWIN = [Ctr(es, nc, f"stWIN{i}") for i in range(2)]
        stWINf = Ctr(es, nc, "stWINf")
        sVa2 = Ctr(es, nc, "sVa2")
        stW1 = [Ctr(es, nc, f"stW1_{i}") for i in range(2)]
        stW1f = Ctr(es, nc, "stW1f")
        stW2 = [Ctr(es, nc, f"stW2_{i}") for i in range(2)]
        stW2f = Ctr(es, nc, "stW2f")
        sVh = Ctr(es, nc, "sVh")
        sVy = Ctr(es, nc, "sVy")
        BARSEM = [sQ, sG, sSc, sPE, sV, sA, stWIN[0], stWIN[1], stWINf, sVa2,
                  stW1[0], stW1[1], stW1f, stW2[0], stW2[1], stW2f,
                  sVh, sVy]

        esAB = es.enter_context(contextlib.ExitStack())
        xnT = [esAB.enter_context(nc.sbuf_tensor(f"xnT{dc}", [128, TOK + 6], BF16))
               for dc in range(8)]
        s_mcs = esAB.enter_context(nc.sbuf_tensor("c_mcs", [32, TOK + 1], F32))

        block = es.enter_context(nc.Block())

        def wait_all(eng):
            for s in BARSEM:
                if s.n:
                    eng.wait_ge(s.h, s.n)

        def barrier():
            for opener in (block.sync, block.gpsimd, block.vector,
                           block.scalar, block.tensor):
                def body(e):
                    wait_all(e)
                opener(body)

        def transpose_evac(items, idt):
            pe0, v0 = sPE.n, sV.n
            ptile = p_tpb if idt is BF16 else p_tpf
            idap = s_idbf if idt is BF16 else s_id

            def tbody(t):
                for i, (src, dst, rows, cols) in enumerate(items):
                    t.wait_ge(sV.h, v0 + i)
                    sPE.inc(t.transpose(ptile[0:cols, 0:rows], src,
                                        idap[0:rows, 0:rows]))
            block.tensor(tbody)

            def vbody(v):
                for i, (src, dst, rows, cols) in enumerate(items):
                    v.wait_ge(sPE.h, pe0 + i + 1)
                    sV.inc(v.tensor_copy(dst, ptile[0:cols, 0:rows]))
            block.vector(vbody)

        def s0_sync(sync):
            for dst, src in [
                (s_id, ident), (s_idbf, identbf),
                (s_cw["f"], cw["f"]), (s_cw["r"], cw["r"]),
                (s_cb["f"], cb["f"]), (s_cb["r"], cb["r"]),
                (s_db["f"], dbias["f"]), (s_db["r"], dbias["r"]),
                (s_eA["f"], alog["f"]), (s_eA["r"], alog["r"]),
                (s_dcol["f"], dcol["f"]), (s_dcol["r"], dcol["r"]),
                (s_nw["f"], nwin["f"]), (s_nw["r"], nwin["r"]),
                (s_mcs, maskcs), (s_b1c, b1c), (s_bu, bu1),
                (s_m8["f"], m8["f"]), (s_m8["r"], m8["r"]),
                (s_mk2["f"], mk2["f"]), (s_mk2["r"], mk2["r"]),
            ]:
                dmaq(sQ, sync, out=dst[:, :], in_=src)
        block.sync(s0_sync)

        def s0_scalar(s):
            wait_all(s)
            for d in DIRS:
                sA.inc(s.activation(s_eA[d][:, :], s_eA[d][:, :], AF.Exp))
        block.scalar(s0_scalar)

        nc.const_aps.register(F32, EPS, s_cvals[:, 0:1])
        nc.const_aps.register(F32, 1.0, s_cvals[:, 1:2])

        def s0_vec(v):
            wait_all(v)
            v.memset(s_cvals[:, 0:1], EPS)
            v.memset(s_cvals[:, 1:2], 1.0)
            sV.inc(v.memset(s_ones[:, :], 1.0))
            for d in DIRS:
                v.memset(s_S[d][:, :], 0.0)
                sV.inc(v.memset(s_absum[d][:, :], 0.0))
        block.vector(s0_vec)

        with contextlib.ExitStack() as ps:
            def psb(n, shp, dt=F32):
                return ps.enter_context(nc.sbuf_tensor(n, shp, dt))

            stg = [psb(f"wstg{i}", [128, 4608]) for i in range(2)]
            stgb = [psb(f"wstgb{i}", [128, 4608], BF16) for i in range(2)]

            jobs = []
            for dc in range(8):
                for c0, c1 in [(0, 2048), (2048, 4096), (4096, NPROJ)]:
                    jobs.append((wiT[dc * 128:(dc + 1) * 128, c0:c1],
                                 ("win", c0 // 128, dc), c1 - c0, None))
            for e in range(4):
                for dc in range(8):
                    jobs.append((w1[e, dc * 128:(dc + 1) * 128, :],
                                 ("w1", e, dc), 4096, None))
            for e in range(4):
                for fc in range(32):
                    jobs.append((w2[e, fc * 128:(fc + 1) * 128, :],
                                 ("w2", e, fc), 1024, None))
            for d in DIRS:
                for pc in range(16):
                    jobs.append((woT[pc * 128:(pc + 1) * 128, :],
                                 ("wo", d, pc), 1024, (d, pc)))
            NJ = len(jobs)
            q0, v0, g0 = sQ.n, sV.n, sG.n

            def pre_sync(sync):
                for i, (src, dk, ncol, sc_) in enumerate(jobs):
                    if i >= 2:
                        sync.wait_ge(sV.h, v0 + (i - 1))
                    dmaq(sQ, sync, out=stg[i % 2][:, 0:ncol], in_=src)
                sync.wait_ge(sV.h, v0 + NJ - 1)
                dmaq(sQ, sync, out=stg[0][0:1, 0:4096], in_=b2r)
            block.sync(pre_sync)

            def pre_vec(v):
                for i, (src, dk, ncol, sc_) in enumerate(jobs):
                    v.wait_ge(sQ.h, q0 + 16 * (i + 1))
                    if i >= 2:
                        v.wait_ge(sG.h, g0 + 16 * (i - 1))
                    if sc_ is None:
                        ncp = ((ncol + 127) // 128) * 128
                        sV.inc(v.tensor_copy(stgb[i % 2][:, 0:ncp],
                                             stg[i % 2][:, 0:ncp]))
                    else:
                        d, pc = sc_
                        sV.inc(v.tensor_scalar(
                            stgb[i % 2][:, 0:1024], stg[i % 2][:, 0:1024],
                            s_nw[d][:, pc:pc + 1], None, ALU.mult))
                v.wait_ge(sQ.h, q0 + 16 * (NJ + 1))
                sV.inc(v.tensor_copy(s_b2bf[:, :], stg[0][0:1, 0:4096]))
            block.vector(pre_vec)

            def pre_g(g):
                for i, (src, dk, ncol, sc_) in enumerate(jobs):
                    g.wait_ge(sV.h, v0 + (i + 1))
                    kind = dk[0]
                    if kind == "win":
                        _, cc0, dc = dk
                        ncc = (ncol + 127) // 128
                        dmaq(sG, g, 
                            out=d_winbf[cc0:cc0 + ncc, dc, :, :]
                            .rearrange("a p c -> p a c"),
                            in_=stgb[i % 2][:, 0:ncc * 128]
                            .rearrange("p (a c) -> p a c", c=128))
                    elif kind == "w1":
                        _, e, dc = dk
                        dmaq(sG, g, 
                            out=d_w1bf[e, :, dc, :, :].rearrange("a p c -> p a c"),
                            in_=stgb[i % 2][:, 0:4096]
                            .rearrange("p (a c) -> p a c", c=128))
                    elif kind == "w2":
                        _, e, fc = dk
                        dmaq(sG, g, 
                            out=d_w2bf[e, :, fc, :, :].rearrange("a p c -> p a c"),
                            in_=stgb[i % 2][:, 0:1024]
                            .rearrange("p (a c) -> p a c", c=128))
                    else:
                        _, d, pc = dk
                        dmaq(sG, g, 
                            out=d_wobf[d][:, pc, :, :].rearrange("a p c -> p a c"),
                            in_=stgb[i % 2][:, 0:1024]
                            .rearrange("p (a c) -> p a c", c=128))
            block.gpsimd(pre_g)
            barrier()

        with contextlib.ExitStack() as ps:
            def psb(n, shp, dt=F32):
                return ps.enter_context(nc.sbuf_tensor(n, shp, dt))

            def pps(n, shp, dt=F32):
                return ps.enter_context(nc.psum_tensor(n, shp, dt))

            p_tpb = pps("p_tpb", [128, 128], BF16)
            p_tpf = pps("p_tpf", [128, 128])
            esA1 = ps.enter_context(contextlib.ExitStack())
            x_t = [esA1.enter_context(nc.sbuf_tensor(f"x_t{i}", [128, DM], F32))
                   for i in range(2)]
            xsq = esA1.enter_context(nc.sbuf_tensor("xsq", [128, DM], F32))
            redc = esA1.enter_context(nc.sbuf_tensor("redc", [128, 4], F32))
            xnb = esA1.enter_context(nc.sbuf_tensor("xnb", [128, DM], BF16))
            wu_rep = esA1.enter_context(nc.sbuf_tensor("wu_rep", [128, DM], F32))
            ucol1 = esA1.enter_context(nc.sbuf_tensor("ucol1", [128, 1], F32))
            udot = esA1.enter_context(nc.sbuf_tensor("udot", [128, DM], F32))

            def a1_wu(sync):
                dmaq(sQ, sync, out=wu_rep[:, :],
                                      in_=wu_row.broadcast_to([128, DM]))
            block.sync(a1_wu)

            ln_jobs = [("m", 3 + ci * 128, 128) for ci in range(16)] + \
                      [("h", 0, 3), ("h", 2051, 3)]
            for i, (kind, r0, rows) in enumerate(ln_jobs):
                xt = x_t[i % 2]

                def a1_sync(sync, xt=xt, r0=r0, rows=rows, i=i):
                    if i >= 1:
                        sync.wait_ge(sV.h, sV.n)
                    dmaq(sQ, sync, out=xt[0:rows, :], in_=xin[r0:r0 + rows, :])
                block.sync(a1_sync)

                def a1_v1(v, xt=xt, rows=rows):
                    v.wait_ge(sQ.h, sQ.n)
                    v.tensor_reduce(redc[0:rows, 0:1], xt[0:rows, :], AXX, ALU.add)
                    v.drain()
                    sV.inc(v.tensor_scalar_mul(redc[0:rows, 2:3], redc[0:rows, 0:1],
                                               1.0 / DM))
                block.vector(a1_v1)

                def a1_s1(s, xt=xt, rows=rows):
                    s.wait_ge(sQ.h, sQ.n)
                    sA.inc(s.activation(xsq[0:rows, :], xt[0:rows, :], AF.Square))
                block.scalar(a1_s1)

                def a1_v2(v, rows=rows):
                    v.wait_ge(sA.h, sA.n)
                    v.tensor_reduce(redc[0:rows, 1:2], xsq[0:rows, :], AXX, ALU.add)
                    v.drain()
                    v.tensor_tensor(redc[0:rows, 3:4], redc[0:rows, 2:3],
                                    redc[0:rows, 2:3], ALU.mult)
                    v.drain()
                    sV.inc(v.scalar_tensor_tensor(redc[0:rows, 3:4],
                                                  redc[0:rows, 1:2], 1.0 / DM,
                                                  redc[0:rows, 3:4],
                                                  ALU.mult, ALU.subtract))
                block.vector(a1_v2)

                def a1_s2(s, rows=rows):
                    s.wait_ge(sV.h, sV.n)
                    s.activation(redc[0:rows, 3:4], redc[0:rows, 3:4], AF.Ln,
                                 bias=EPS)
                    s.drain()
                    sA.inc(s.activation(redc[0:rows, 3:4], redc[0:rows, 3:4],
                                        AF.Exp, scale=-0.5))
                block.scalar(a1_s2)

                def a1_v3(v, xt=xt, rows=rows, kind=kind):
                    v.wait_ge(sA.h, sA.n)
                    v.wait_ge(sG.h, sG.n)
                    v.tensor_scalar(xsq[0:rows, :], xt[0:rows, :],
                                    redc[0:rows, 2:3], redc[0:rows, 3:4],
                                    ALU.subtract, ALU.mult)
                    v.drain()
                    if kind == "m":
                        v.tensor_tensor(udot[0:rows, :], xsq[0:rows, :],
                                        wu_rep[0:rows, :], ALU.mult)
                        v.tensor_reduce(ucol1[0:rows, :], udot[0:rows, :], AXX,
                                        ALU.add)
                        v.drain()
                    sV.inc(v.tensor_copy(xnb[0:rows, :], xsq[0:rows, :]))
                block.vector(a1_v3)

                if kind == "m":
                    def a1_gu(g, ci=(r0 - 3) // 128):
                        g.wait_ge(sV.h, sV.n)
                        dmaq(sG, g, 
                            out=d_uraw[0:1, ci * 128:(ci + 1) * 128]
                            .rearrange("one t -> t one"),
                            in_=ucol1[:, :])
                        if DEBUG and ci == 0:
                            dmaq(sG, g, out=dbg["o_redc"], in_=redc[:, :])
                            dmaq(sG, g, out=dbg["o_xsq"], in_=xsq[:, :])
                            dmaq(sG, g, out=dbg["o_xnb"], in_=xnb[:, :])
                            dmaq(sG, g, out=dbg["o_udot"], in_=udot[:, :])
                    block.gpsimd(a1_gu)

                transpose_evac(
                    [(xnb[0:rows, dc * 128:(dc + 1) * 128],
                      xnT[dc][:, r0:r0 + rows], rows, 128) for dc in range(8)],
                    BF16)

            barrier()
            esA1.close()

            wbuf = [psb(f"wbuf{i}", [128, 1024], BF16) for i in range(2)]
            p_ipm = [pps(f"p_ipm{i}", [128, 512]) for i in range(2)]
            p_ips = [pps(f"p_ips{i}", [128, 8]) for i in range(2)]
            zstage = [psb(f"zstage{i}", [128, 512]) for i in range(2)]
            xbcsc = [psb(f"xbcsc{pc}", [128, 518]) for pc in range(18)]
            dtusc = psb("dtusc", [33, 518])
            convt = [psb(f"convt{i}", [128, 512]) for i in range(2)]
            xsst = [psb(f"xsst{i}", [128, 512]) for i in range(2)]
            xsbf = [psb(f"xsbf{pc}", [128, 512], BF16) for pc in range(16)]
            bcbf = psb("bcbf", [128, 1024], BF16)
            dtp = {d: psb(f"dtp{d}", [32, 512]) for d in DIRS}
            abp = {d: psb(f"abp{d}", [32, 512]) for d in DIRS}
            acu = {d: psb(f"acu{d}", [32, 512]) for d in DIRS}
            ddcp = {d: psb(f"ddcp{d}", [32, 512]) for d in DIRS}
            scr = [psb(f"scrA{i}", [32, 512]) for i in range(3)]
            erowb = psb("erowb", [32, 512], BF16)
            stk = psb("stk", [64, 128])
            dtddc = psb("dtddc", [128, 64])
            xstm = psb("xstm", [128, DIN], BF16)
            xdt = psb("xdt", [128, DIN], BF16)
            xdp = psb("xdp", [128, DIN], BF16)
            btm = psb("btm", [128, 128], BF16)
            p_cs = pps("p_cs", [128, 512])
            csst = psb("csst", [128, 512], BF16)
            dchrep = psb("dchrep", [128, 32])
            stmp = psb("stmp", [128, 512])

            for s in range(NSC):
                w0 = s * SC
                tok0 = s * SC

                a2q0 = [stWIN[0].n, stWIN[1].n]
                a2f0 = stWINf.n

                def a2_ldr(sync, s=s, a2f0=a2f0):
                    for cc in range(NPC):
                        if a2f0 + cc - 1 > 0:
                            sync.wait_ge(stWINf.h, a2f0 + cc - 1)
                        stWIN[cc % 2].dma(sync.dma_start(
                            out=wbuf[cc % 2][:, :]
                            .rearrange("p (a c) -> p a c", c=128),
                            in_=d_winbf[cc, :, :, :].rearrange("a p c -> p a c")))
                block.sync(a2_ldr)

                v2b = sVa2.n

                def a2_pe(t, s=s, w0=w0, a2q0=a2q0, v2b=v2b):
                    for cc in range(NPC):
                        t.wait_ge(stWIN[cc % 2].h,
                                  a2q0[cc % 2] + 16 * (cc // 2 + 1))
                        if v2b + cc - 1 > 0:
                            t.wait_ge(sVa2.h, v2b + cc - 1)
                        for dc in range(8):
                            lh = wbuf[cc % 2][:, dc * 128:(dc + 1) * 128]
                            t.matmul(p_ipm[cc % 2][:, :], lh,
                                     xnT[dc][:, w0:w0 + 512],
                                     start=(dc == 0), stop=(dc == 7))
                            ins = t.matmul(p_ips[cc % 2][:, 0:6], lh,
                                           xnT[dc][:, w0 + 512:w0 + 518],
                                           start=(dc == 0), stop=(dc == 7))
                            if dc == 7:
                                stWINf.inc(ins)
                block.tensor(a2_pe)

                for cc in range(NPC):
                    pb = cc % 2
                    if cc < 16:
                        def a2_vz(v, pb=pb, cc=cc, a2f0=a2f0):
                            v.wait_ge(stWINf.h, a2f0 + cc + 1)
                            v.wait_ge(sG.h, sG.n)
                            v.tensor_copy(zstage[pb][:, 0:509], p_ipm[pb][:, 3:512])
                            sVa2.inc(v.tensor_copy(zstage[pb][:, 509:512],
                                                   p_ips[pb][:, 0:3]))
                        block.vector(a2_vz)

                        def a2_gz(g, cc=cc, pb=pb, tok0=tok0):
                            g.wait_ge(sVa2.h, sVa2.n)
                            dmaq(sG, g, out=d_z[cc * 128:(cc + 1) * 128,
                                                tok0:tok0 + 512],
                                 in_=zstage[pb][:, :])
                        block.gpsimd(a2_gz)
                    elif cc < 34:
                        def a2_vx(v, cc=cc, pb=pb, a2f0=a2f0):
                            v.wait_ge(stWINf.h, a2f0 + cc + 1)
                            pc = cc - 16
                            v.tensor_copy(xbcsc[pc][:, 0:512], p_ipm[pb][:, :])
                            sVa2.inc(v.tensor_copy(xbcsc[pc][:, 512:518],
                                                   p_ips[pb][:, 0:6]))
                        block.vector(a2_vx)
                    else:
                        def a2_vd(v, pb=pb, cc=cc, a2f0=a2f0):
                            v.wait_ge(stWINf.h, a2f0 + cc + 1)
                            v.tensor_copy(dtusc[0:33, 0:512], p_ipm[pb][0:33, :])
                            sVa2.inc(v.tensor_copy(dtusc[0:33, 512:518],
                                                   p_ips[pb][0:33, 0:6]))
                        block.vector(a2_vd)

                def a2_join(v):
                    v.wait_ge(sVa2.h, sVa2.n)
                    sV.inc(v.tensor_copy(scr[0][:, 0:1], dtusc[0:33, 0:1][0:32, :]))
                block.vector(a2_join)

                def a2_gu(g, tok0=tok0):
                    g.wait_ge(sV.h, sV.n)
                    if DEBUG and tok0 == 1536:
                        dmaq(sG, g, out=dbg["o_xbc16"], in_=xbcsc[16][:, :])
                        dmaq(sG, g, out=dbg["o_xbc17"], in_=xbcsc[17][:, :])
                block.gpsimd(a2_gu)

                for d in DIRS:
                    def a3_s1(sE, d=d):
                        sE.wait_ge(sV.h, sV.n)
                        sE.activation(scr[0][:, :], dtusc[0:32, 3:515], AF.Abs,
                                      bias=s_db[d][:, :])
                        sE.drain()
                        sE.activation(scr[0][:, :], scr[0][:, :], AF.Exp, scale=-1.0)
                        sE.drain()
                        sA.inc(sE.activation(scr[0][:, :], scr[0][:, :], AF.Ln,
                                             bias=1.0))
                    block.scalar(a3_s1)

                    def a3_v1(v, d=d, tok0=tok0):
                        v.wait_ge(sA.h, sA.n)
                        v.tensor_scalar(scr[1][:, :], dtusc[0:32, 3:515],
                                        s_db[d][:, :], 0.0, ALU.add, ALU.max)
                        v.tensor_tensor(dtp[d][:, :], scr[1][:, :], scr[0][:, :],
                                        ALU.add)
                        v.tensor_scalar(abp[d][:, :], dtp[d][:, :], s_eA[d][:, :],
                                        -1.0, ALU.mult, ALU.mult)
                        v.tensor_reduce(scr[2][:, 0:1], abp[d][:, :], AXX, ALU.add)
                        v.drain()
                        v.tensor_tensor(s_absum[d][:, :], s_absum[d][:, :],
                                        scr[2][:, 0:1], ALU.add)
                        sV.inc(v.tensor_tensor_scan(acu[d][:, :],
                                                    s_mcs[:, tok0:tok0 + 512],
                                                    abp[d][:, :], 0.0,
                                                    ALU.mult, ALU.add))
                    block.vector(a3_v1)

                    def a3_s2(sE, d=d):
                        sE.wait_ge(sV.h, sV.n)
                        sA.inc(sE.activation(scr[0][:, :], abp[d][:, :], AF.Exp))
                    block.scalar(a3_s2)

                    def a3_v2(v, d=d, tok0=tok0):
                        v.wait_ge(sA.h, sA.n)
                        moff = tok0 if d == "f" else tok0 + 1
                        sV.inc(v.tensor_tensor(scr[1][:, :], scr[0][:, :],
                                               s_mcs[:, moff:moff + 512], ALU.mult))
                    block.vector(a3_v2)

                    def a3_g1(g, d=d, tok0=tok0):
                        g.wait_ge(sV.h, sV.n)
                        dmaq(sG, g, out=d_rrow[d][:, tok0:tok0 + 512],
                                           in_=scr[1][:, :])
                    block.gpsimd(a3_g1)

                    totv = acu[d][:, 127::128].unsqueeze(2).broadcast_to([32, 4, 128])
                    acv = acu[d].rearrange("h (c t) -> h c t", c=4)
                    if d == "f":
                        def a3_ef(sE, d=d):
                            sE.wait_ge(sV.h, sV.n)
                            sE.wait_ge(sG.h, sG.n)
                            sA.inc(sE.activation(erowb[:, :], acu[d][:, :], AF.Exp))
                        block.scalar(a3_ef)

                        def a3_vdec(v, totv=totv, acv=acv):
                            v.wait_ge(sA.h, sA.n)
                            v.wait_ge(sG.h, sG.n)
                            sV.inc(v.tensor_tensor(
                                scr[1].rearrange("h (c t) -> h c t", c=4),
                                totv, acv, ALU.subtract))
                        block.vector(a3_vdec)

                        def a3_sdec(sE):
                            sE.wait_ge(sV.h, sV.n)
                            sA.inc(sE.activation(scr[2][:, :], scr[1][:, :], AF.Exp))
                        block.scalar(a3_sdec)
                    else:
                        def a3_vr(v, d=d, totv=totv, acv=acv):
                            v.wait_ge(sG.h, sG.n)
                            v.tensor_tensor(scr[1][:, :], acu[d][:, :],
                                            abp[d][:, :], ALU.subtract)
                            v.tensor_tensor(
                                scr[2].rearrange("h (c t) -> h c t", c=4),
                                totv, acv, ALU.subtract)
                            sV.inc(v.tensor_tensor(scr[2][:, :], scr[2][:, :],
                                                   abp[d][:, :], ALU.add))
                        block.vector(a3_vr)

                        def a3_sr(sE):
                            sE.wait_ge(sV.h, sV.n)
                            sE.wait_ge(sG.h, sG.n)
                            sE.activation(erowb[:, :], scr[2][:, :], AF.Exp)
                            sE.drain()
                            sA.inc(sE.activation(scr[2][:, :], scr[1][:, :], AF.Exp))
                        block.scalar(a3_sr)

                    def a3_g2(g, d=d, tok0=tok0):
                        g.wait_ge(sA.h, sA.n)
                        dmaq(sG, g, out=d_erow[d][:, tok0:tok0 + 512],
                                           in_=erowb[:, :])
                    block.gpsimd(a3_g2)

                    def a3_v3(v, d=d):
                        v.wait_ge(sA.h, sA.n)
                        sV.inc(v.tensor_tensor(ddcp[d][:, :], dtp[d][:, :],
                                               scr[2][:, :], ALU.mult))
                    block.vector(a3_v3)

                    def a3_s3(sE, d=d):
                        sE.wait_ge(sV.h, sV.n)
                        sA.inc(sE.activation(scr[2][:, 0:4], acu[d][:, 127::128],
                                             AF.Exp))
                    block.scalar(a3_s3)

                    def a3_g3(g, d=d, s=s):
                        g.wait_ge(sA.h, sA.n)
                        for c4i in range(4):
                            dmaq(sG, g, 
                                out=d_dch[d][s * 4 + c4i:s * 4 + c4i + 1, :]
                                .rearrange("one h -> h one"),
                                in_=scr[2][:, c4i:c4i + 1])
                    block.gpsimd(a3_g3)

                for d in DIRS:
                    for pc in range(18):
                        def a4_v(v, d=d, pc=pc):
                            v.wait_ge(sV.h, sV.n)
                            v.wait_ge(sA.h, sA.n)
                            if pc < 16:
                                v.wait_ge(sG.h, sG.n)
                            k0off = 0 if d == "f" else 6
                            v.tensor_scalar(convt[0][:, :],
                                            xbcsc[pc][:, k0off:k0off + 512],
                                            s_cw[d][:, pc * 4:pc * 4 + 1],
                                            s_cb[d][:, pc:pc + 1],
                                            ALU.mult, ALU.add)
                            cur = 0
                            for k in (1, 2, 3):
                                off = k if d == "f" else 6 - k
                                ins = v.scalar_tensor_tensor(
                                    convt[1 - cur][:, :],
                                    xbcsc[pc][:, off:off + 512],
                                    s_cw[d][:, pc * 4 + k:pc * 4 + k + 1],
                                    convt[cur][:, :], ALU.mult, ALU.add)
                                cur = 1 - cur
                            sV.inc(ins)
                        block.vector(a4_v)

                        def a4_s(sE, d=d, pc=pc):
                            sE.wait_ge(sV.h, sV.n)
                            sE.wait_ge(sG.h, sG.n)
                            if pc < 16:
                                sA.inc(sE.activation(xsst[pc % 2][:, :],
                                                     convt[1][:, :], AF.Silu))
                            else:
                                half = pc - 16
                                sA.inc(sE.activation(
                                    bcbf[:, half * 512:(half + 1) * 512],
                                    convt[1][:, :], AF.Silu))
                        block.scalar(a4_s)

                        if pc < 16:
                            def a4_v2(v, d=d, pc=pc):
                                v.wait_ge(sA.h, sA.n)
                                sV.inc(v.tensor_copy(xsbf[pc][:, :],
                                                     xsst[pc % 2][:, :]))
                            block.vector(a4_v2)

                            def a4_g(g, d=d, pc=pc, tok0=tok0):
                                g.wait_ge(sA.h, sA.n)
                                dmaq(sG, g, 
                                    out=d_xscm[d][pc * 128:(pc + 1) * 128,
                                                  tok0:tok0 + 512],
                                    in_=xsst[pc % 2][:, :])
                            block.gpsimd(a4_g)

                    def a4_gbc(g, d=d, tok0=tok0):
                        g.wait_ge(sA.h, sA.n)
                        dmaq(sG, g, out=d_B[d][:, tok0:tok0 + 512],
                                           in_=bcbf[:, 0:512])
                        dmaq(sG, g, out=d_C[d][:, tok0:tok0 + 512],
                                           in_=bcbf[:, 512:1024])
                    block.gpsimd(a4_gbc)

                    for cl in range(4):
                        ci = s * 4 + cl
                        t0 = cl * 128
                        def a5_v1(v, d=d, t0=t0):
                            v.wait_ge(sV.h, sV.n)
                            v.wait_ge(sG.h, sG.n)
                            v.tensor_copy(stk[0:32, :], dtp[d][:, t0:t0 + 128])
                            sV.inc(v.tensor_copy(stk[32:64, :],
                                                 ddcp[d][:, t0:t0 + 128]))
                        block.vector(a5_v1)

                        transpose_evac([(stk[0:64, :], dtddc[:, 0:64], 64, 128)],
                                       F32)

                        transpose_evac(
                            [(xsbf[pc][:, t0:t0 + 128],
                              xstm[:, pc * 128:(pc + 1) * 128], 128, 128)
                             for pc in range(16)], BF16)

                        def a5_v2(v, d=d):
                            v.wait_ge(sG.h, sG.n)
                            dtv = dtddc[:, 0:32].unsqueeze(2) \
                                .broadcast_to([128, 32, 64])
                            ddv = dtddc[:, 32:64].unsqueeze(2) \
                                .broadcast_to([128, 32, 64])
                            xv = xstm.rearrange("t (h p) -> t h p", h=32)
                            v.tensor_tensor(xdt.rearrange("t (h p) -> t h p", h=32),
                                            xv, dtv, ALU.mult)
                            sV.inc(v.tensor_tensor(
                                xdp.rearrange("t (h p) -> t h p", h=32),
                                xv, ddv, ALU.mult))
                        block.vector(a5_v2)

                        def a5_g1(g, d=d, ci=ci):
                            g.wait_ge(sV.h, sV.n)
                            dmaq(sG, g, out=d_xd[d][ci, :, :], in_=xdt[:, :])
                        block.gpsimd(a5_g1)

                        transpose_evac([(bcbf[:, t0:t0 + 128], btm[:, :],
                                         128, 128)], BF16)

                        def a5_sync(sync, d=d, ci=ci):
                            sync.wait_ge(sG.h, sG.n)
                            dmaq(sQ, sync, 
                                out=dchrep[:, :],
                                in_=d_dch[d][ci:ci + 1, :]
                                .broadcast_to([128, 32]))
                        block.sync(a5_sync)

                        for g4 in range(4):
                            def a5_t(t, g4=g4):
                                t.wait_ge(sV.h, sV.n)
                                t.wait_ge(sQ.h, sQ.n)
                                for h8 in range(8):
                                    h = g4 * 8 + h8
                                    ins = t.matmul(p_cs[:, h8 * 64:(h8 + 1) * 64],
                                                   btm[:, :],
                                                   xdp[:, h * 64:(h + 1) * 64],
                                                   start=True, stop=True)
                                    if h8 == 7:
                                        sPE.inc(ins)
                            block.tensor(a5_t)

                            def a5_v3(v, d=d, g4=g4):
                                v.wait_ge(sPE.h, sPE.n)
                                v.wait_ge(sG.h, sG.n)
                                gc = slice(g4 * 512, (g4 + 1) * 512)
                                v.tensor_copy(csst[:, :], p_cs[:, :])
                                dv = dchrep[:, g4 * 8:(g4 + 1) * 8].unsqueeze(2) \
                                    .broadcast_to([128, 8, 64])
                                v.tensor_tensor(
                                    stmp.rearrange("n (h p) -> n h p", h=8),
                                    s_S[d][:, gc].rearrange("n (h p) -> n h p", h=8),
                                    dv, ALU.mult)
                                sV.inc(v.tensor_tensor(s_S[d][:, gc], stmp[:, :],
                                                       p_cs[:, :], ALU.add))
                            block.vector(a5_v3)

                            def a5_g2(g, d=d, ci=ci, g4=g4):
                                g.wait_ge(sV.h, sV.n)
                                dmaq(sG, g, 
                                    out=d_cs[d][ci, :, g4 * 512:(g4 + 1) * 512],
                                    in_=csst[:, :])
                            block.gpsimd(a5_g2)

            for d in DIRS:
                def a6_s(sE, d=d):
                    wait_all(sE)
                    sA.inc(sE.activation(scr[2][:, 0:1], s_absum[d][:, :], AF.Exp))
                block.scalar(a6_s)

                def a6_g(g, d=d):
                    g.wait_ge(sA.h, sA.n)
                    r0 = 0 if d == "f" else 128
                    dmaq(sG, g, out=cc_in[r0:r0 + 128, :], in_=s_S[d][:, :])
                    c0 = 0 if d == "f" else 32
                    dmaq(sG, g, 
                        out=cc_in[256:257, c0:c0 + 32]
                        .rearrange("one h -> (one h)").unsqueeze(1),
                        in_=scr[2][:, 0:1])
                block.gpsimd(a6_g)

            def a6_cc(g):
                wait_all(g)
                sCC.inc(g.collective_compute(
                    "AllGather", ALU.bypass,
                    replica_groups=[list(range(NCORES))],
                    ins=[cc_in.opt()], outs=[cc_out.opt()]))
            block.gpsimd(a6_cc)
            barrier()

        with contextlib.ExitStack() as ps:
            def psb(n, shp, dt=F32):
                return ps.enter_context(nc.sbuf_tensor(n, shp, dt))

            def pps(n, shp, dt=F32):
                return ps.enter_context(nc.psum_tensor(n, shp, dt))

            cbc = [psb(f"cbc{e}", [128, TOK], BF16) for e in range(4)]
            esB0 = ps.enter_context(contextlib.ExitStack())
            urow = esB0.enter_context(nc.sbuf_tensor("urow", [1, TOK], F32))
            u4 = esB0.enter_context(nc.sbuf_tensor("u4", [4, TOK], F32))
            thr = esB0.enter_context(nc.sbuf_tensor("thr", [4, TOK], F32))
            rec = esB0.enter_context(nc.sbuf_tensor("rec", [4, TOK], F32))
            c4 = esB0.enter_context(nc.sbuf_tensor("c4", [4, TOK], F32))
            c4b = esB0.enter_context(nc.sbuf_tensor("c4b", [4, TOK], BF16))

            def b0_sync(sync):
                dmaq(sQ, sync, out=urow[:, :], in_=d_uraw[0:1, :])
                dmaq(sQ, sync, out=thr[:, :], in_=thr4)
                dmaq(sQ, sync, out=rec[:, :], in_=rec4)
            block.sync(b0_sync)

            def b0_s(sE):
                sE.wait_ge(sQ.h, sQ.n)
                sA.inc(sE.activation(urow[:, :], urow[:, :], AF.Sigmoid,
                                     bias=s_bu[:, :]))
            block.scalar(b0_s)

            def b0_g(g):
                g.wait_ge(sA.h, sA.n)
                dmaq(sG, g, out=d_u, in_=urow[:, :])
            block.gpsimd(b0_g)

            def b0_sync2(sync):
                sync.wait_ge(sG.h, sG.n)
                dmaq(sQ, sync, out=u4[:, :], in_=d_u.broadcast_to([4, TOK]))
            block.sync(b0_sync2)

            def b0_v(v):
                v.wait_ge(sQ.h, sQ.n)
                v.tensor_tensor(c4[:, :], u4[:, :], thr[:, :], ALU.is_gt)
                v.tensor_tensor(c4[:, :], c4[:, :], u4[:, :], ALU.mult)
                v.tensor_tensor(c4[:, :], c4[:, :], rec[:, :], ALU.mult)
                sV.inc(v.tensor_copy(c4b[:, :], c4[:, :]))
            block.vector(b0_v)

            def b0_g2(g):
                g.wait_ge(sV.h, sV.n)
                dmaq(sG, g, out=d_c4, in_=c4b[:, :])
            block.gpsimd(b0_g2)

            def b0_sync3(sync):
                sync.wait_ge(sG.h, sG.n)
                for e in range(4):
                    dmaq(sQ, sync, 
                        out=cbc[e][:, :],
                        in_=d_c4[e:e + 1, :].broadcast_to([128, TOK]))
            block.sync(b0_sync3)
            barrier()
            esB0.close()

            w1g = [psb(f"w1g{i}", [128, 1024], BF16) for i in range(2)]
            w2g = [psb(f"w2g{i}", [128, 4096], BF16) for i in range(2)]
            hbuf = [psb(f"hbuf{fc}", [128, 512], BF16) for fc in range(32)]
            hst = psb("hst", [128, 512])
            yacc = [psb(f"yaccB{dc}", [128, 512]) for dc in range(8)]
            gst = psb("gstB", [128, 512])
            p_h = [pps(f"p_h{i}", [128, 512]) for i in range(2)]
            p_y = [pps(f"p_yB{i}", [128, 512]) for i in range(2)]

            for tt in range(4):
                w0 = 3 + tt * 512
                for e in range(4):
                    q0 = [stW1[0].n, stW1[1].n]
                    f0, vh0 = stW1f.n, sVh.n

                    def b1_ldr(sE, e=e, q0=q0, f0=f0):
                        for fc in range(32):
                            if f0 + fc - 1 > 0:
                                sE.wait_ge(stW1f.h, f0 + fc - 1)
                            stW1[fc % 2].dma(sE.dma_start(
                                out=w1g[fc % 2][:, :]
                                .rearrange("p (a c) -> p a c", c=128),
                                in_=d_w1bf[e, fc, :, :, :]
                                .rearrange("a p c -> p a c")))
                    block.scalar(b1_ldr)

                    def b1_pe(t, e=e, w0=w0, q0=q0, vh0=vh0):
                        for fc in range(32):
                            t.wait_ge(stW1[fc % 2].h,
                                      q0[fc % 2] + 16 * (fc // 2 + 1))
                            if vh0 + fc - 1 > 0:
                                t.wait_ge(sVh.h, vh0 + fc - 1)
                            for dc in range(8):
                                ins = t.matmul(p_h[fc % 2][:, :],
                                               w1g[fc % 2][:, dc * 128:(dc + 1) * 128],
                                               xnT[dc][:, w0:w0 + 512],
                                               start=(dc == 0), stop=(dc == 7))
                                if dc == 7:
                                    stW1f.inc(ins)
                    block.tensor(b1_pe)

                    def b1_ev(v, e=e, tt=tt, f0=f0):
                        for fc in range(32):
                            v.wait_ge(stW1f.h, f0 + fc + 1)
                            v.tensor_scalar(hst[:, :], p_h[fc % 2][:, :],
                                            s_b1c[:, e * 32 + fc:e * 32 + fc + 1],
                                            0.0, ALU.add, ALU.max)
                            sVh.inc(v.tensor_tensor(
                                hbuf[fc][:, :], hst[:, :],
                                cbc[e][:, tt * 512:(tt + 1) * 512], ALU.mult))
                    block.vector(b1_ev)

                    q2 = [stW2[0].n, stW2[1].n]
                    f2, vy0 = stW2f.n, sVy.n
                    vh1 = sVh.n

                    def b1_ldr2(sE, e=e, q2=q2, f2=f2):
                        for dc in range(8):
                            if f2 + dc - 1 > 0:
                                sE.wait_ge(stW2f.h, f2 + dc - 1)
                            stW2[dc % 2].dma(sE.dma_start(
                                out=w2g[dc % 2][:, :]
                                .rearrange("p (a c) -> p a c", c=128),
                                in_=d_w2bf[e, dc, :, :, :]
                                .rearrange("a p c -> p a c")))
                    block.scalar(b1_ldr2)

                    def b1_pe2(t, e=e, tt=tt, q2=q2, vy0=vy0, vh1=vh1):
                        for dc in range(8):
                            t.wait_ge(stW2[dc % 2].h,
                                      q2[dc % 2] + 16 * (dc // 2 + 1))
                            if dc == 0:
                                t.wait_ge(sVh.h, vh1)
                            if vy0 + dc - 1 > 0:
                                t.wait_ge(sVy.h, vy0 + dc - 1)
                            for fc in range(32):
                                t.matmul(p_y[dc % 2][:, :],
                                         w2g[dc % 2][:, fc * 128:(fc + 1) * 128],
                                         hbuf[fc][:, :],
                                         start=(fc == 0), stop=False)
                            stW2f.inc(t.matmul(
                                p_y[dc % 2][:, :],
                                s_b2bf[0:1, e * 1024 + dc * 128:
                                       e * 1024 + (dc + 1) * 128],
                                cbc[e][0:1, tt * 512:(tt + 1) * 512],
                                start=False, stop=True))
                    block.tensor(b1_pe2)

                    def b1_ev2(v, e=e, f2=f2):
                        for dc in range(8):
                            v.wait_ge(stW2f.h, f2 + dc + 1)
                            if e == 0:
                                v.wait_ge(sA.h, sA.n)
                                sVy.inc(v.tensor_copy(yacc[dc][:, :],
                                                      p_y[dc % 2][:, :]))
                            else:
                                sVy.inc(v.tensor_tensor(yacc[dc][:, :],
                                                        yacc[dc][:, :],
                                                        p_y[dc % 2][:, :], ALU.add))
                    block.vector(b1_ev2)

                for dc in range(8):
                    def b1_sg(sE, dc=dc, tt=tt):
                        sE.wait_ge(sVy.h, sVy.n)
                        sE.wait_ge(sG.h, sG.n)
                        sA.inc(sE.activation(gst[:, :], yacc[dc][:, :], AF.Silu))
                    block.scalar(b1_sg)

                    def b1_gg(g, dc=dc, tt=tt):
                        g.wait_ge(sA.h, sA.n)
                        dmaq(sG, g, out=d_g[dc * 128:(dc + 1) * 128,
                                            tt * 512:(tt + 1) * 512],
                             in_=gst[:, :])
                    block.gpsimd(b1_gg)

            barrier()

            drows = psb("drows", [8, 64])
            lnd = psb("lnd", [8, 64])
            w8 = psb("w8", [8, 64])
            w8rep = psb("w8rep", [128, 64])
            sjt = psb("sjt", [128, DIN])
            p_w8 = pps("p_w8", [8, 64])

            def b2_sync(sync):
                sync.wait_ge(sCC.h, 1)
                dmaq(sQ, sync, 
                    out=drows[:, :],
                    in_=cc_out.rearrange("(j r) c -> j r c", r=257)[:, 256, 0:64])
            block.sync(b2_sync)

            def b2_s(sE):
                sE.wait_ge(sQ.h, sQ.n)
                sA.inc(sE.activation(lnd[:, :], drows[:, :], AF.Ln))
            block.scalar(b2_s)

            def b2_vc(v):
                v.wait_ge(sA.h, sA.n)
                sV.inc(v.tensor_scalar_max(lnd[:, :], lnd[:, :], -1e4))
            block.vector(b2_vc)

            def b2_t(t):
                t.wait_ge(sV.h, sV.n)
                t.matmul(p_w8[:, 0:32], s_mk2["f"][:, :], lnd[:, 0:32],
                         start=True, stop=True)
                sPE.inc(t.matmul(p_w8[:, 32:64], s_mk2["r"][:, :], lnd[:, 32:64],
                                 start=True, stop=True))
            block.tensor(b2_t)

            def b2_s2(sE):
                sE.wait_ge(sPE.h, sPE.n)
                sA.inc(sE.activation(w8[:, :], p_w8[:, :], AF.Exp))
            block.scalar(b2_s2)

            def b2_v(v):
                v.wait_ge(sA.h, sA.n)
                v.tensor_scalar(w8[:, 0:32], w8[:, 0:32], s_m8["f"][:, :],
                                None, ALU.mult)
                v.tensor_scalar(w8[:, 32:64], w8[:, 32:64], s_m8["r"][:, :],
                                None, ALU.mult)
                for d in DIRS:
                    sV.inc(v.memset(s_S[d][:, :], 0.0))
            block.vector(b2_v)

            def b2_g(g):
                g.wait_ge(sV.h, sV.n)
                dmaq(sG, g, out=d_w8, in_=w8[:, :])
            block.gpsimd(b2_g)

            for j in range(NCORES):
                def b2_sj(sync, j=j):
                    sync.wait_ge(sG.h, sG.n)
                    sync.wait_ge(sV.h, sV.n)
                    dmaq(sQ, sync, 
                        out=w8rep[:, :],
                        in_=d_w8[j:j + 1, :].broadcast_to([128, 64]))
                block.sync(b2_sj)

                for d in DIRS:
                    r0 = 257 * j + (0 if d == "f" else 128)
                    c0 = 0 if d == "f" else 32

                    def b2_sjl(sync, r0=r0):
                        sync.wait_ge(sV.h, sV.n)
                        dmaq(sQ, sync, out=sjt[:, :],
                                              in_=cc_out[r0:r0 + 128, :])
                    block.sync(b2_sjl)

                    def b2_vj(v, d=d, c0=c0):
                        v.wait_ge(sQ.h, sQ.n)
                        wv = w8rep[:, c0:c0 + 32].unsqueeze(2) \
                            .broadcast_to([128, 32, 64])
                        v.tensor_tensor(sjt.rearrange("n (h p) -> n h p", h=32),
                                        sjt.rearrange("n (h p) -> n h p", h=32),
                                        wv, ALU.mult)
                        sV.inc(v.tensor_tensor(s_S[d][:, :], s_S[d][:, :],
                                               sjt[:, :], ALU.add))
                    block.vector(b2_vj)

            if DEBUG:
                def bdbg(g):
                    wait_all(g)
                    g.wait_ge(sCC.h, 1)
                    dmaq(sG, g, out=dbg["o_inc_f"], in_=s_S["f"][:, :])
                    dmaq(sG, g, out=dbg["o_inc_r"], in_=s_S["r"][:, :])
                    dmaq(sG, g, out=dbg["o_xn0"], in_=xnT[0][:, :])
                block.gpsimd(bdbg)
            barrier()

        esAB.close()

        with contextlib.ExitStack() as ps:
            def psb(n, shp, dt=F32):
                return ps.enter_context(nc.sbuf_tensor(n, shp, dt))

            def pps(n, shp, dt=F32):
                return ps.enter_context(nc.psum_tensor(n, shp, dt))

            cst = [psb(f"Ccst{i}", [128, DIN], BF16) for i in range(2)]
            s_it = psb("c_itile", [128, DIN])
            dchrep = psb("Cdchrep", [128, 32])
            srst = psb("Csrst", [128, DIN], BF16)
            stmp = psb("Cstmp", [128, 512])

            def cld_it(sync):
                dmaq(sQ, sync, out=s_it[:, :], in_=itile)
            block.sync(cld_it)

            for k, ci in enumerate(range(NCH - 1, -1, -1)):
                def c0_sync(sync, ci=ci, k=k):
                    if k >= 1:
                        sync.wait_ge(sV.h, sV.n)
                    dmaq(sQ, sync, out=cst[k % 2][:, :],
                                          in_=d_cs["r"][ci, :, :])
                    dmaq(sQ, sync, 
                        out=dchrep[:, :],
                        in_=d_dch["r"][ci:ci + 1, :]
                        .broadcast_to([128, 32]))
                block.sync(c0_sync)

                def c0_v(v, ci=ci, k=k):
                    v.wait_ge(sQ.h, sQ.n)
                    if k > 0:
                        v.wait_ge(sG.h, sG.n)
                    cp = v.tensor_copy(srst[:, :], s_S["r"][:, :])
                    if ci > 0:
                        for g4 in range(4):
                            gc = slice(g4 * 512, (g4 + 1) * 512)
                            dv = dchrep[:, g4 * 8:(g4 + 1) * 8].unsqueeze(2) \
                                .broadcast_to([128, 8, 64])
                            v.tensor_tensor(
                                stmp[:, 0:512].rearrange("n (h p) -> n h p", h=8),
                                s_S["r"][:, gc].rearrange("n (h p) -> n h p", h=8),
                                dv, ALU.mult)
                            ins = v.tensor_tensor(s_S["r"][:, gc], stmp[:, 0:512],
                                                  cst[k % 2][:, gc], ALU.add)
                        sV.inc(ins)
                    else:
                        sV.inc(cp)
                block.vector(c0_v)

                def c0_g(g, ci=ci):
                    g.wait_ge(sV.h, sV.n)
                    dmaq(sG, g, out=d_Sr[ci, :, :], in_=srst[:, :])
                block.gpsimd(c0_g)
            barrier()

            zt = psb("Czt", [128, DIN])
            szt = psb("Cszt", [128, DIN])
            xscm = {d: psb(f"Cxscm{d}", [128, DIN]) for d in DIRS}
            xdc = {d: psb(f"Cxd{d}", [128, DIN], BF16) for d in DIRS}
            srl = psb("Csrl", [128, DIN], BF16)
            sfb = psb("Csfb", [128, DIN], BF16)
            bcl = {d: psb(f"Cbc{d}", [128, 256], BF16) for d in DIRS}
            rbc = psb("Crbc", [128, 2048])
            ebc = psb("Cebc", [128, 2048], BF16)
            Rt = psb("CR", [128, 2048])
            wpp = psb("Cwpp", [128, 4096], BF16)
            wyo = psb("Cwyo", [128, 4096], BF16)
            gpr = psb("Cgpr", [128, 128])
            ych = {d: psb(f"Cych{d}", [128, DIN]) for d in DIRS}
            gbuf = {d: psb(f"Cgbuf{d}", [128, 8192], BF16) for d in DIRS}
            ysq = psb("Cysq", [128, DIN], BF16)
            rstdb = {d: psb(f"Crstdb{d}", [128, 512]) for d in DIRS}
            wog = psb("Cwog", [128, 2048], BF16)
            gl = [psb(f"Cgl{i}", [128, 512]) for i in range(2)]
            sgb = [None] * 8
            for _dc in range(8):
                _d = "f" if _dc < 4 else "r"
                sgb[_dc] = ych[_d][:, (_dc % 4) * 512:(_dc % 4 + 1) * 512]
            xot = psb("Cxot", [128, DM])
            outsb = psb("Coutsb", [128, DM])
            nurow = psb("Cnurow", [1, 128])

            p_g = pps("p_g", [128, 128])
            p_yc = [pps(f"p_yc{i}", [128, 128]) for i in range(2)]
            p_nu = pps("p_nu", [1, 128])
            p_s = {d: pps(f"p_s{d}", [128, 512]) for d in DIRS}
            p_tr = pps("p_tr", [128, DM])

            for grp in range(4):
                for cl in range(4):
                    ci = grp * 4 + cl
                    tok0 = ci * T
                    for d in DIRS:
                        def c1_ld(sync, d=d, ci=ci, tok0=tok0):
                            wait_all(sync)
                            dmaq(sQ, sync, 
                                out=xscm[d][:, :]
                                .rearrange("p (pc t) -> p pc t", pc=16),
                                in_=d_xscm[d]
                                .rearrange("(pc p) t -> p pc t", p=128)
                                [:, :, tok0:tok0 + 128])
                            dmaq(sQ, sync, out=xdc[d][:, :],
                                                  in_=d_xd[d][ci, :, :])
                            dmaq(sQ, sync, out=bcl[d][:, 0:128],
                                                  in_=d_B[d][:, tok0:tok0 + 128])
                            dmaq(sQ, sync, out=bcl[d][:, 128:256],
                                                  in_=d_C[d][:, tok0:tok0 + 128])
                            dmaq(sQ, sync, 
                                out=rbc[:, :]
                                .rearrange("p (h t) -> p h t", h=16),
                                in_=d_rrow[d][0:16, tok0:tok0 + 128]
                                .unsqueeze(0).broadcast_to([128, 16, 128]))
                            dmaq(sQ, sync, 
                                out=ebc[:, :]
                                .rearrange("p (h t) -> p h t", h=16),
                                in_=d_erow[d][0:16, tok0:tok0 + 128]
                                .unsqueeze(0).broadcast_to([128, 16, 128]))
                            dmaq(sQ, sync, 
                                out=dchrep[:, :],
                                in_=d_dch[d][ci:ci + 1, :]
                                .broadcast_to([128, 32]))
                            if d == "f":
                                dmaq(sQ, sync, out=cst[0][:, :],
                                                      in_=d_cs["f"][ci, :, :])
                                dmaq(sQ, sync, 
                                    out=zt[:, :]
                                    .rearrange("p (pc t) -> p pc t", pc=16),
                                    in_=d_z.rearrange("(pc p) t -> p pc t", p=128)
                                    [:, :, tok0:tok0 + 128])
                            else:
                                dmaq(sQ, sync, out=srl[:, :],
                                                      in_=d_Sr[ci, :, :])
                        block.sync(c1_ld)

                        def c1_t1(t, d=d):
                            t.wait_ge(sQ.h, sQ.n)
                            sPE.inc(t.matmul(p_g[:, :], bcl[d][:, 0:128],
                                             bcl[d][:, 128:256],
                                             start=True, stop=True))
                        block.tensor(c1_t1)

                        gv16 = gpr[:, :].unsqueeze(1).broadcast_to([128, 16, 128])

                        def c1_v1a(v, d=d):
                            v.wait_ge(sPE.h, sPE.n)
                            v.tensor_copy(gpr[:, :], p_g[:, :])
                            if d == "f":
                                v.tensor_tensor_scan(Rt[:, :], rbc[:, :],
                                                     s_it[:, :], 0.0,
                                                     ALU.mult, ALU.add)
                            else:
                                v.tensor_tensor_scan(Rt[:, ::-1], rbc[:, ::-1],
                                                     s_it[:, ::-1], 0.0,
                                                     ALU.mult, ALU.add)
                            v.tensor_tensor(
                                wpp[:, 0:2048].rearrange("s (h l) -> s h l", h=16),
                                gv16,
                                Rt.rearrange("s (h l) -> s h l", h=16), ALU.mult)
                            cv16 = bcl[d][:, 128:256].unsqueeze(1) \
                                .broadcast_to([128, 16, 128])
                            sV.inc(v.tensor_tensor(
                                wyo[:, 0:2048].rearrange("n (h l) -> n h l", h=16),
                                cv16,
                                ebc.rearrange("n (h l) -> n h l", h=16), ALU.mult))
                        block.vector(c1_v1a)

                        def c1_s1b(sync, d=d, tok0=tok0):
                            sync.wait_ge(sV.h, sV.n)
                            dmaq(sQ, sync, 
                                out=rbc[:, :]
                                .rearrange("p (h t) -> p h t", h=16),
                                in_=d_rrow[d][16:32, tok0:tok0 + 128]
                                .unsqueeze(0).broadcast_to([128, 16, 128]))
                            dmaq(sQ, sync, 
                                out=ebc[:, :]
                                .rearrange("p (h t) -> p h t", h=16),
                                in_=d_erow[d][16:32, tok0:tok0 + 128]
                                .unsqueeze(0).broadcast_to([128, 16, 128]))
                        block.sync(c1_s1b)

                        def c1_v1b(v, d=d):
                            v.wait_ge(sQ.h, sQ.n)
                            if d == "f":
                                v.tensor_tensor_scan(Rt[:, :], rbc[:, :],
                                                     s_it[:, :], 0.0,
                                                     ALU.mult, ALU.add)
                            else:
                                v.tensor_tensor_scan(Rt[:, ::-1], rbc[:, ::-1],
                                                     s_it[:, ::-1], 0.0,
                                                     ALU.mult, ALU.add)
                            v.tensor_tensor(
                                wpp[:, 2048:4096].rearrange("s (h l) -> s h l", h=16),
                                gv16,
                                Rt.rearrange("s (h l) -> s h l", h=16), ALU.mult)
                            cv16 = bcl[d][:, 128:256].unsqueeze(1) \
                                .broadcast_to([128, 16, 128])
                            v.tensor_tensor(
                                wyo[:, 2048:4096].rearrange("n (h l) -> n h l", h=16),
                                cv16,
                                ebc.rearrange("n (h l) -> n h l", h=16), ALU.mult)
                            if d == "f":
                                sV.inc(v.tensor_copy(sfb[:, :], s_S["f"][:, :]))
                            else:
                                sV.inc(v.tensor_copy(gpr[0:1, 0:1], gpr[0:1, 0:1]))
                        block.vector(c1_v1b)

                        for hp in range(16):
                            pb = hp % 2

                            def c1_t2(t, d=d, hp=hp, pb=pb):
                                t.wait_ge(sV.h, sV.n)
                                Sop = sfb if d == "f" else srl
                                for sub in range(2):
                                    h = hp * 2 + sub
                                    r0 = sub * 64
                                    t.matmul(p_yc[pb][r0:r0 + 64, :],
                                             xdc[d][:, h * 64:(h + 1) * 64],
                                             wpp[:, h * 128:(h + 1) * 128],
                                             start=True, stop=False)
                                    ins = t.matmul(p_yc[pb][r0:r0 + 64, :],
                                                   Sop[:, h * 64:(h + 1) * 64],
                                                   wyo[:, h * 128:(h + 1) * 128],
                                                   start=False, stop=True)
                                    if sub == 1:
                                        sPE.inc(ins)
                            block.tensor(c1_t2)

                            def c1_v2(v, d=d, hp=hp, pb=pb):
                                v.wait_ge(sPE.h, sPE.n)
                                sV.inc(v.scalar_tensor_tensor(
                                    ych[d][:, hp * 128:(hp + 1) * 128],
                                    xscm[d][:, hp * 128:(hp + 1) * 128],
                                    s_dcol[d][:, hp:hp + 1],
                                    p_yc[pb][:, :], ALU.mult, ALU.add))
                            block.vector(c1_v2)

                        if d == "f":
                            def c1_vS(v):
                                for g4 in range(4):
                                    gc = slice(g4 * 512, (g4 + 1) * 512)
                                    dv = dchrep[:, g4 * 8:(g4 + 1) * 8] \
                                        .unsqueeze(2).broadcast_to([128, 8, 64])
                                    v.tensor_tensor(
                                        stmp.rearrange("n (h p) -> n h p", h=8),
                                        s_S["f"][:, gc]
                                        .rearrange("n (h p) -> n h p", h=8),
                                        dv, ALU.mult)
                                    ins = v.tensor_tensor(s_S["f"][:, gc],
                                                          stmp[:, :],
                                                          cst[0][:, gc], ALU.add)
                                sV.inc(ins)
                            block.vector(c1_vS)

                    if DEBUG and ci == 0:
                        def cdbg(g):
                            g.wait_ge(sV.h, sV.n)
                            dmaq(sG, g, out=dbg["o_ych_f"], in_=ych["f"][:, :])
                            dmaq(sG, g, out=dbg["o_ych_r"], in_=ych["r"][:, :])
                        block.gpsimd(cdbg)

                    def c1_sz(sE):
                        sE.wait_ge(sV.h, sV.n)
                        sA.inc(sE.activation(szt[:, :], zt[:, :], AF.Silu))
                    block.scalar(c1_sz)

                    for d in DIRS:
                        def c1_vg(v, d=d, cl=cl):
                            v.wait_ge(sA.h, sA.n)
                            gv = gbuf[d].rearrange("p (pc t) -> p pc t", pc=16) \
                                [:, :, cl * 128:(cl + 1) * 128]
                            yv = ych[d].rearrange("p (pc t) -> p pc t", pc=16)
                            zv = szt.rearrange("p (pc t) -> p pc t", pc=16)
                            v.tensor_tensor(gv, yv, zv, ALU.mult)
                            sV.inc(v.tensor_tensor(
                                ysq.rearrange("p (pc t) -> p pc t", pc=16),
                                gv, gv, ALU.mult))
                        block.vector(c1_vg)

                        def c1_tn(t, d=d):
                            t.wait_ge(sV.h, sV.n)
                            t.wait_ge(sA.h, sA.n)
                            for pc in range(16):
                                ins = t.matmul(p_nu[:, :], s_ones[:, :],
                                               ysq[:, pc * 128:(pc + 1) * 128],
                                               start=(pc == 0), stop=(pc == 15))
                            sPE.inc(ins)
                        block.tensor(c1_tn)

                        def c1_sn(sE, d=d):
                            sE.wait_ge(sPE.h, sPE.n)
                            sE.wait_ge(sG.h, sG.n)
                            sE.activation(nurow[:, :], p_nu[:, :], AF.Ln,
                                          scale=1.0 / DIN, bias=EPS)
                            sE.drain()
                            sA.inc(sE.activation(nurow[:, :], nurow[:, :],
                                                 AF.Exp, scale=-0.5))
                        block.scalar(c1_sn)

                        def c1_gn(g, d=d, tok0=tok0):
                            g.wait_ge(sA.h, sA.n)
                            dmaq(sG, g, out=d_rstd[d][0:1, tok0:tok0 + 128],
                                               in_=nurow[:, :])
                        block.gpsimd(c1_gn)

                g0 = grp * 512

                def ce_ld(sync, g0=g0):
                    wait_all(sync)
                    for d in DIRS:
                        dmaq(sQ, sync, 
                            out=rstdb[d][:, :],
                            in_=d_rstd[d][0:1, g0:g0 + 512]
                            .broadcast_to([128, 512]))
                    for i in range(2):
                        dmaq(sQ, sync, 
                            out=gl[i][:, :],
                            in_=d_g[i * 128:(i + 1) * 128, g0:g0 + 512])
                block.sync(ce_ld)

                for dc in range(8):
                    for d in DIRS:
                        def ce_ws(sE, d=d, dc=dc):
                            sE.wait_ge(sPE.h, sPE.n)
                            dmaq(sSc, sE, 
                                out=wog[:, :]
                                .rearrange("p (a c) -> p a c", c=128),
                                in_=d_wobf[d][dc, :, :, :]
                                .rearrange("a p c -> p a c"))
                        block.scalar(ce_ws)

                        def ce_t(t, d=d):
                            t.wait_ge(sSc.h, sSc.n)
                            t.wait_ge(sQ.h, sQ.n)
                            t.wait_ge(sV.h, sV.n)
                            for pc in range(16):
                                ins = t.matmul(p_s[d][:, :],
                                               wog[:, pc * 128:(pc + 1) * 128],
                                               gbuf[d][:, pc * 512:(pc + 1) * 512],
                                               start=(pc == 0), stop=(pc == 15))
                            sPE.inc(ins)
                        block.tensor(ce_t)

                    def ce_gl(sync, dc=dc, g0=g0):
                        if dc + 2 < 8:
                            sync.wait_ge(sV.h, sV.n)
                            dmaq(sQ, sync, 
                                out=gl[dc % 2][:, :],
                                in_=d_g[(dc + 2) * 128:(dc + 3) * 128,
                                        g0:g0 + 512]) if False else None
                    def ce_v(v, dc=dc):
                        v.wait_ge(sPE.h, sPE.n)
                        v.wait_ge(sQ.h, sQ.n)
                        v.tensor_tensor(sgb[dc], p_s["f"][:, :],
                                        rstdb["f"][:, :], ALU.mult)
                        v.scalar_tensor_tensor(stmp[:, :], p_s["r"][:, :],
                                               1.0, rstdb["r"][:, :],
                                               ALU.mult, ALU.mult)
                        v.tensor_tensor(sgb[dc], sgb[dc],
                                        stmp[:, :], ALU.add)
                        sV.inc(v.tensor_tensor(sgb[dc], sgb[dc],
                                               gl[dc % 2][:, :], ALU.mult))
                    block.vector(ce_v)

                    def ce_gl2(sync, dc=dc, g0=g0):
                        if dc + 2 < 8:
                            sync.wait_ge(sV.h, sV.n)
                            dmaq(sQ, sync, 
                                out=gl[dc % 2][:, :],
                                in_=d_g[(dc + 2) * 128:(dc + 3) * 128,
                                        g0:g0 + 512])
                    block.sync(ce_gl2)

                for tsub in range(4):
                    trow = g0 + tsub * 128

                    def ce_x(sync, trow=trow):
                        sync.wait_ge(sV.h, sV.n)
                        dmaq(sQ, sync, out=xot[:, :],
                                              in_=xin[3 + trow:3 + trow + 128, :])
                    block.sync(ce_x)

                    def ce_tr(t, tsub=tsub):
                        t.wait_ge(sQ.h, sQ.n)
                        t.wait_ge(sV.h, sV.n)
                        for dc in range(8):
                            ins = t.transpose(p_tr[:, dc * 128:(dc + 1) * 128],
                                              sgb[dc][:, tsub * 128:(tsub + 1) * 128],
                                              s_id[:, :])
                        sPE.inc(ins)
                    block.tensor(ce_tr)

                    def ce_va(v):
                        v.wait_ge(sPE.h, sPE.n)
                        v.wait_ge(sG.h, sG.n)
                        sV.inc(v.tensor_tensor(outsb[:, :], xot[:, :],
                                               p_tr[:, :], ALU.add))
                    block.vector(ce_va)

                    def ce_go(g, trow=trow):
                        g.wait_ge(sV.h, sV.n)
                        dmaq(sG, g, out=out[trow:trow + 128, :],
                                           in_=outsb[:, :])
                    block.gpsimd(ce_go)

            if DEBUG:
                def cdump(g):
                    wait_all(g)
                    dmaq(sG, g, out=dbg["o_z"], in_=d_z[0:256, :])
                    dmaq(sG, g, out=dbg["o_rrow_f"], in_=d_rrow["f"])
                    dmaq(sG, g, out=dbg["o_rrow_r"], in_=d_rrow["r"])
                    dmaq(sG, g, out=dbg["o_erow_f"], in_=d_erow["f"])
                    dmaq(sG, g, out=dbg["o_erow_r"], in_=d_erow["r"])
                    dmaq(sG, g, out=dbg["o_dch_f"], in_=d_dch["f"])
                    dmaq(sG, g, out=dbg["o_dch_r"], in_=d_dch["r"])
                    dmaq(sG, g, out=dbg["o_xd_f"], in_=d_xd["f"][0, :, :])
                    dmaq(sG, g, out=dbg["o_xd_r"], in_=d_xd["r"][0, :, :])
                    dmaq(sG, g, out=dbg["o_cs_f"], in_=d_cs["f"][0, :, :])
                    dmaq(sG, g, out=dbg["o_cs_r"], in_=d_cs["r"][0, :, :])
                    dmaq(sG, g, out=dbg["o_xscm_f"], in_=d_xscm["f"][0:256, :])
                    dmaq(sG, g, out=dbg["o_B_f"], in_=d_B["f"])
                    dmaq(sG, g, out=dbg["o_C_f"], in_=d_C["f"])
                    dmaq(sG, g, out=dbg["o_ccin"], in_=cc_in)
                    dmaq(sG, g, out=dbg["o_u"], in_=d_u)
                    dmaq(sG, g, out=dbg["o_c4"], in_=d_c4)
                    dmaq(sG, g, out=dbg["o_g"], in_=d_g[0:256, :])
                    dmaq(sG, g, out=dbg["o_rstd_f"], in_=d_rstd["f"])
                    dmaq(sG, g, out=dbg["o_rstd_r"], in_=d_rstd["r"])
                    dmaq(sG, g, out=dbg["o_sr15"], in_=d_Sr[15, :, :])
                block.gpsimd(cdump)

            def c_end(g):
                g.wait_ge(sG.h, sG.n)
            block.gpsimd(c_end)

    return nc


_NC_CACHE = []
_LAST = [None]


def kernel(L0_attn_x, L0_attn_W_in, L0_attn_W_out,
           L0_attn_conv_w_f, L0_attn_conv_b_f, L0_attn_dt_bias_f,
           L0_attn_A_log_f, L0_attn_D_f, L0_attn_norm_w_f,
           L0_attn_conv_w_r, L0_attn_conv_b_r, L0_attn_dt_bias_r,
           L0_attn_A_log_r, L0_attn_D_r, L0_attn_norm_w_r,
           L0_ffn_W1, L0_ffn_b1, L0_ffn_W2, L0_ffn_b2, L0_ffn_Wu, L0_ffn_bu):
    import ml_dtypes
    f32 = np.float32
    x = np.asarray(L0_attn_x, f32)[0]
    W_in = np.asarray(L0_attn_W_in, f32)
    Wu = np.asarray(L0_ffn_Wu, f32)
    wiT = np.ascontiguousarray(np.concatenate([W_in.T, Wu], axis=1))
    woT = np.ascontiguousarray(np.asarray(L0_attn_W_out, f32).T)

    def convpack(w):
        return np.ascontiguousarray(np.asarray(w, f32).reshape(18, 128, 4)
                                    .transpose(1, 0, 2).reshape(128, 72))

    def biaspack(b):
        return np.ascontiguousarray(np.asarray(b, f32).reshape(18, 128).T)

    def colpack(v):
        return np.ascontiguousarray(np.asarray(v, f32).reshape(32, 1))

    def dpack(Dv):
        return np.ascontiguousarray(
            np.repeat(np.asarray(Dv, f32), 64).reshape(16, 128).T)

    def nwpack(nw):
        return np.ascontiguousarray(np.asarray(nw, f32).reshape(16, 128).T)

    b1cv = np.ascontiguousarray(np.asarray(L0_ffn_b1, f32)
                                .reshape(4, 32, 128).transpose(2, 0, 1)
                                .reshape(128, 128))
    b2rv = np.ascontiguousarray(np.asarray(L0_ffn_b2, f32).reshape(1, 4096))
    bu1v = np.ascontiguousarray(np.asarray(L0_ffn_bu, f32).reshape(1, 1))

    tloc = np.arange(TOK)
    thr4v = np.empty((4, TOK), f32)
    rec4v = np.empty((4, TOK), f32)
    for e in range(4):
        i_et = ((e - tloc) % 4) + 1
        thr4v[e] = (i_et - 1) / 4.0
        rec4v[e] = 1.0 / i_et
    maskv = np.ones((32, TOK + 1), f32)
    maskv[:, ::T] = 0.0
    identv = np.eye(128, dtype=f32)
    identbfv = identv.astype(ml_dtypes.bfloat16)
    itilev = np.ascontiguousarray(np.tile(identv, (1, 16)))

    if not _NC_CACHE:
        _NC_CACHE.append(build_program())
    nc = _NC_CACHE[0]

    in_maps = []
    for c in range(NCORES):
        lo = c * TOK
        xhv = np.zeros((TOK + 6, DM), f32)
        s0 = max(0, lo - 3)
        e0 = min(SEQ, lo + TOK + 3)
        xhv[s0 - (lo - 3):e0 - (lo - 3)] = x[s0:e0]
        m8fv = np.zeros((8, 1), f32); m8fv[:c] = 1.0
        m8rv = np.zeros((8, 1), f32); m8rv[c + 1:] = 1.0
        mk2fv = np.zeros((8, 8), f32)
        mk2rv = np.zeros((8, 8), f32)
        for j in range(8):
            for m in range(8):
                if j < m < c:
                    mk2fv[m, j] = 1.0
                if c < m < j:
                    mk2rv[m, j] = 1.0
        in_maps.append(dict(
            xin=xhv, wiT=wiT,
            w1=np.ascontiguousarray(np.asarray(L0_ffn_W1, f32)),
            w2=np.ascontiguousarray(np.asarray(L0_ffn_W2, f32)),
            b1c=b1cv, b2r=b2rv, woT=woT,
            cwf=convpack(L0_attn_conv_w_f), cbf=biaspack(L0_attn_conv_b_f),
            cwr=convpack(L0_attn_conv_w_r), cbr=biaspack(L0_attn_conv_b_r),
            dbf=colpack(L0_attn_dt_bias_f), alf=colpack(L0_attn_A_log_f),
            dbr=colpack(L0_attn_dt_bias_r), alr=colpack(L0_attn_A_log_r),
            dcolf=dpack(L0_attn_D_f), dcolr=dpack(L0_attn_D_r),
            nwf=nwpack(L0_attn_norm_w_f), nwr=nwpack(L0_attn_norm_w_r),
            bu1=bu1v, wu_row=np.ascontiguousarray(Wu.reshape(1, DM)),
            thr4=thr4v, rec4=rec4v, maskcs=maskv,
            ident=identv, identbf=identbfv, itile=itilev,
            m8f=m8fv, m8r=m8rv, mk2f=mk2fv, mk2r=mk2rv,
        ))

    res = run_bass_kernel_spmd(nc, in_maps, core_ids=list(range(NCORES)))
    _LAST[0] = res
    pieces = [np.asarray(res.results[c]["out"]) for c in range(NCORES)]
    return np.concatenate(pieces, axis=0)[None, :, :].astype(np.float32)
